# version 01
# baseline (speedup 1.0000x reference)
import numpy as np
import ml_dtypes
from contextlib import ExitStack

import concourse.bass as bass
import concourse.mybir as mybir
from concourse.bass_utils import run_bass_kernel_spmd

F32 = mybir.dt.float32
BF16 = mybir.dt.bfloat16
AF = mybir.ActivationFunctionType
ALU = mybir.AluOpType
AX = mybir.AxisListType
MAGIC = 12582912.0
NC = 8
TOK = 2048
DM = 1024
INNER = 4096
DIN = 2048
DST = 16
BT = 8192


class Prog:

    def __init__(self, nc, ctx):
        self.nc = nc
        self.queues = {e: [] for e in ("sync", "gpsimd", "vector", "scalar", "tensor")}
        self.sems = {}
        self.cnt = {}
        self._ctx = ctx
        self._dsems = []
        for name in ("sv", "ss", "sg", "st"):
            self.sems[name] = ctx.enter_context(nc.semaphore(name))
            self.cnt[name] = 0

    def _emit(self, e, fn):
        self.queues[e].append(fn)

    def wait(self, e, sname, tok):
        if tok is None or tok <= 0:
            return
        sem = self.sems[sname]
        self._emit(e, lambda eng, sem=sem, tok=tok: eng.wait_ge(sem, tok))

    def op(self, e, meth, *args, **kw):
        sname = {"vector": "sv", "scalar": "ss", "gpsimd": "sg"}[e]
        sem = self.sems[sname]
        pre = self.cnt[sname]
        self.cnt[sname] = pre + 1

        def fn(eng, meth=meth, args=args, kw=kw, pre=pre, sem=sem):
            if pre:
                eng.wait_ge(sem, pre)
            getattr(eng, meth)(*args, **kw).then_inc(sem, 1)
        self._emit(e, fn)
        return self.cnt[sname]

    def v(self, meth, *a, **kw):
        return self.op("vector", meth, *a, **kw)

    def s(self, meth, *a, **kw):
        return self.op("scalar", meth, *a, **kw)

    def g(self, meth, *a, **kw):
        return self.op("gpsimd", meth, *a, **kw)

    def t(self, meth, *args, inc=False, **kw):
        sem = self.sems["st"]
        if inc:
            self.cnt["st"] += 1
            self._emit("tensor", lambda eng, meth=meth, args=args, kw=kw, sem=sem:
                       getattr(eng, meth)(*args, **kw).then_inc(sem, 1))
            return self.cnt["st"]
        self._emit("tensor", lambda eng, meth=meth, args=args, kw=kw:
                   getattr(eng, meth)(*args, **kw))
        return None

    def dsem(self, name):
        if name not in self.sems:
            self.sems[name] = self._ctx.enter_context(self.nc.semaphore(name))
            self.cnt[name] = 0
            self._dsems.append(name)
        return self.sems[name]

    def dma(self, e, out, in_, sem):
        s = self.dsem(sem)
        self.cnt[sem] += 16
        tok = self.cnt[sem]
        self._emit(e, lambda eng, out=out, in_=in_, s=s:
                   eng.dma_start(out=out, in_=in_).then_inc(s, 16))
        return tok

    def dwait(self, e, sem):
        self.dsem(sem)
        self.wait(e, sem, self.cnt[sem])

    def drain_all(self):
        for sname in self._dsems:
            self.wait("sync", sname, self.cnt[sname])

    def finish(self, block):
        for e, fns in self.queues.items():
            if not fns:
                continue
            dec = getattr(block, e)

            def body(eng, fns=tuple(fns)):
                for f in fns:
                    f(eng)
            dec(body)


def _mk_consts(p, nc, ctx, need_idb16=True):
    ec = ctx.enter_context
    ones_r = ec(nc.sbuf_tensor("ones_r", [1, 128], F32))
    ones_c = ec(nc.sbuf_tensor("ones_c", [128, 1], F32))
    id32 = ec(nc.sbuf_tensor("id32", [128, 128], F32))
    rowi = ec(nc.sbuf_tensor("rowi", [128, 128], F32))
    coli = ec(nc.sbuf_tensor("coli", [128, 128], F32))
    c = {}
    p.v("memset", ones_r[:], 1.0)
    p.v("memset", ones_c[:], 1.0)
    p.g("iota", rowi[:], [[1, 128]], channel_multiplier=0,
        allow_small_or_imprecise_dtypes=True)
    tg = p.g("iota", coli[:], [[0, 128]], channel_multiplier=1,
             allow_small_or_imprecise_dtypes=True)
    p.wait("vector", "sg", tg)
    c["tok_id"] = p.v("tensor_tensor", id32[:], rowi[:], coli[:], ALU.is_equal)
    c["ones_r"], c["ones_c"], c["id32"] = ones_r, ones_c, id32
    if need_idb16:
        idb = ec(nc.sbuf_tensor("idb16", [128, 128], BF16))
        c["tok_idb"] = p.v("tensor_copy", idb[:], id32[:])
        c["idb16"] = idb
    return c


def _ln_stats(p, x_sub, stats, junk, sv_tok=None):
    F = x_sub.shape[-1]
    p.v("reduce_sum", stats[:, 0:1], x_sub, AX.X)
    if sv_tok is not None:
        p.wait("scalar", "sv", sv_tok)
    tsq = p.s("activation", junk, x_sub, AF.Square, bias=0.0, scale=1.0,
              accum_out=stats[:, 1:2])
    p.v("tensor_scalar", stats[:, 2:3], stats[:, 0:1], 1.0 / F, None, ALU.mult)
    p.wait("vector", "ss", tsq)
    p.v("tensor_scalar", stats[:, 3:4], stats[:, 1:2], 1.0 / F, 1e-5, ALU.mult, ALU.add)
    p.v("tensor_scalar", stats[:, 4:5], stats[:, 2:3], stats[:, 2:3], None, ALU.mult)
    p.v("tensor_tensor", stats[:, 5:6], stats[:, 3:4], stats[:, 4:5], ALU.subtract)
    tr1 = p.v("reciprocal", stats[:, 6:7], stats[:, 5:6])
    p.wait("scalar", "sv", tr1)
    trstd = p.s("activation", stats[:, 7:8], stats[:, 6:7], AF.Sqrt, bias=0.0, scale=1.0)
    p.wait("vector", "ss", trstd)
    tnm = p.v("tensor_scalar", stats[:, 8:9], stats[:, 2:3], stats[:, 7:8], -1.0,
              ALU.mult, ALU.mult)
    return trstd, tnm


def build_A():
    nc = bass.Bass("TRN2", target_bir_lowering=False)
    x_own = nc.dram_tensor("x_own", [TOK, DM], F32, kind="ExternalInput").ap()
    wgT = nc.dram_tensor("wgT", [DM, INNER], F32, kind="ExternalInput").ap()
    wuT = nc.dram_tensor("wuT", [DM, INNER], F32, kind="ExternalInput").ap()
    wdT = nc.dram_tensor("wdT", [INNER, DM], F32, kind="ExternalInput").ap()
    ffn_out = nc.dram_tensor("ffn_out", [TOK, DM], F32, kind="ExternalOutput").ap()
    xn_out = nc.dram_tensor("xn_out", [TOK, DM], F32, kind="ExternalOutput").ap()
    xnsum = nc.dram_tensor("xnsum", [1, DM], F32, kind="ExternalOutput").ap()
    sc_wg = nc.dram_tensor("sc_wg", [DM, INNER], BF16).ap()
    sc_wu = nc.dram_tensor("sc_wu", [DM, INNER], BF16).ap()
    sc_wd = nc.dram_tensor("sc_wd", [INNER, DM], BF16).ap()
    sc_gu = nc.dram_tensor("sc_gu", [TOK, INNER], F32).ap()
    sc_xi2 = nc.dram_tensor("sc_xi2", [INNER, TOK], BF16).ap()

    with ExitStack() as ctx:
        ec = ctx.enter_context
        p = Prog(nc, ctx)
        c = _mk_consts(p, nc, ctx)
        NS = 16

        xi_fm = ec(nc.sbuf_tensor("xi_fm", [128, 8 * TOK], BF16))
        x_t = ec(nc.sbuf_tensor("x_t", [128, 4 * DM], F32))
        xn_sb = ec(nc.sbuf_tensor("xn_sb", [128, 2 * DM], F32))
        sum_sb = ec(nc.sbuf_tensor("sum_sb", [1, DM], F32))
        xi_tm = ec(nc.sbuf_tensor("xi_tm", [128, DM], BF16))
        stats = ec(nc.sbuf_tensor("stats", [128, 24], F32))
        inv_sx = ec(nc.sbuf_tensor("inv_sx", [128, NS], F32))
        sg_vec = ec(nc.sbuf_tensor("sg_vec", [128, NS], F32))
        su_vec = ec(nc.sbuf_tensor("su_vec", [128, NS], F32))
        sd_vec = ec(nc.sbuf_tensor("sd_vec", [128, NS], F32))
        big_f32 = ec(nc.sbuf_tensor("big_f32", [128, 2 * DM], F32))
        ld32 = ec(nc.sbuf_tensor("ld32", [128, INNER], F32))
        wqout = ec(nc.sbuf_tensor("wqout", [128, INNER], BF16))
        wacc = ec(nc.sbuf_tensor("wacc", [128, 4], F32))
        mrow = ec(nc.sbuf_tensor("mrow", [1, 8], F32))
        mbc = ec(nc.sbuf_tensor("mbc", [128, 8], F32))
        wgq_sb = ec(nc.sbuf_tensor("wgq_sb", [128, 8 * 1024], BF16))
        wuq_sb = ec(nc.sbuf_tensor("wuq_sb", [128, 8 * 1024], BF16))
        g_f32 = ec(nc.sbuf_tensor("g_f32", [128, 1024], F32))
        u_f32 = ec(nc.sbuf_tensor("u_f32", [128, 1024], F32))
        gu_f32 = ec(nc.sbuf_tensor("gu_f32", [128, 2 * 1024], F32))
        xi2_tm = ec(nc.sbuf_tensor("xi2_tm", [128, INNER], BF16))
        xi2_stg = ec(nc.sbuf_tensor("xi2_stg", [128, 32 * 128], BF16))
        wd_sb = ec(nc.sbuf_tensor("wd_sb", [128, 2 * 4 * 1024], BF16))
        xd_sb = ec(nc.sbuf_tensor("xd_sb", [128, 2 * 4096], BF16))
        fo_sb = ec(nc.sbuf_tensor("fo_sb", [128, 2 * 1024], F32))

        PS_A = ec(nc.psum_tensor("PS_A", [128, 2048], F32))
        PS_B = ec(nc.psum_tensor("PS_B", [128, 1536], F32))
        PS_T = ec(nc.psum_tensor("PS_T", [128, 1024], BF16))

        block = ec(nc.Block())

        p.v("memset", wacc[:], 0.0)
        p.v("memset", mrow[:], 0.0)
        wsrcs = [(wgT, 8, INNER, 0, 0), (wuT, 8, INNER, 1, 512), (wdT, 32, DM, 2, 1024)]
        tok_abs = 0
        tok_wadd = 0
        for (src, ntile, fdim, col, pcol) in wsrcs:
            for i in range(ntile):
                p.wait("sync", "ss", tok_abs)
                p.dma("sync", ld32[:, 0:fdim], src[i * 128:(i + 1) * 128, :], "wl")
                p.dwait("scalar", "wl")
                p.wait("scalar", "sv", tok_wadd)
                tok_abs = p.s("activation", ld32[:, 0:fdim], ld32[:, 0:fdim],
                              AF.Abs, bias=0.0, scale=1.0, accum_out=stats[:, 16:17])
                p.wait("vector", "ss", tok_abs)
                tok_wadd = p.v("tensor_tensor", wacc[:, col:col + 1],
                               wacc[:, col:col + 1], stats[:, 16:17], ALU.add)
            p.wait("tensor", "sv", p.cnt["sv"])
            p.t("matmul", PS_A[0:1, pcol:pcol + 1], wacc[:, col:col + 1],
                c["ones_c"][:], start=True, stop=True, inc=True)
        p.wait("vector", "st", p.cnt["st"])
        for (_, _, _, col, pcol) in wsrcs:
            n = DM * INNER
            p.v("tensor_scalar", mrow[0:1, col:col + 1], PS_A[0:1, pcol:pcol + 1],
                1.0 / n, 1e-5, ALU.mult, ALU.max)
            p.v("reciprocal", mrow[0:1, col + 3:col + 4], mrow[0:1, col:col + 1])
        tmr = p.cnt["sv"]
        p.wait("tensor", "sv", tmr)
        p.t("matmul", PS_B[:, 0:8], c["ones_r"][:], mrow[0:1, 0:8],
            start=True, stop=True, inc=True)
        p.wait("vector", "st", p.cnt["st"])
        tmbc = p.v("tensor_copy", mbc[:], PS_B[:, 0:8])
        wdsts = [(wgT, sc_wg, 8, INNER, 3), (wuT, sc_wu, 8, INNER, 4),
                 (wdT, sc_wd, 32, DM, 5)]
        tok_wq_dma = 0
        tok_q3 = 0
        p.wait("sync", "ss", tok_abs)
        for (src, dst, ntile, fdim, scol) in wdsts:
            for i in range(ntile):
                p.wait("sync", "sv", tok_q3)
                p.dma("sync", ld32[:, 0:fdim], src[i * 128:(i + 1) * 128, :], "wl")
                p.dwait("vector", "wl")
                p.v("tensor_scalar", ld32[:, 0:fdim], ld32[:, 0:fdim],
                    mbc[:, scol:scol + 1], MAGIC, ALU.mult, ALU.add)
                p.v("tensor_scalar", ld32[:, 0:fdim], ld32[:, 0:fdim],
                    MAGIC, -1.0, ALU.subtract, ALU.max)
                p.dwait("vector", "ws")
                tok_q3 = p.v("tensor_scalar", wqout[:, 0:fdim], ld32[:, 0:fdim],
                             1.0, None, ALU.min)
                p.wait("scalar", "sv", tok_q3)
                p.dma("scalar", dst[i * 128:(i + 1) * 128, :],
                      wqout[:, 0:fdim], "ws")

        xv = x_own.rearrange("(n p) d -> n p d", p=128)
        xnv = xn_out.rearrange("(n p) d -> n p d", p=128)
        x_tv = x_t[:].rearrange("p (s d) -> p s d", d=DM)
        xn_v = xn_sb[:].rearrange("p (s d) -> p s d", d=DM)
        xi_fmv = xi_fm[:].rearrange("p (c t) -> p c t", c=8)
        junk = big_f32[:, 0:DM]
        xitmp = big_f32[:, DM:2 * DM]
        xn_slot_mm = [0, 0]
        xn_slot_dma = [0, 0]
        x_slot_sv = [0, 0, 0, 0]
        x_slot_ss = [0, 0, 0, 0]
        t_colsum = None
        t_xi_last = 0
        for tile in range(4):
            dtoks = []
            for s4 in range(4):
                p.wait("sync", "sv", x_slot_sv[s4])
                p.wait("sync", "ss", x_slot_ss[s4])
                p.dma("sync", x_tv[:, s4, :], xv[tile * 4 + s4], "xl")
            for s4 in range(4):
                sub = tile * 4 + s4
                slot = sub % 2
                x_sub = x_tv[:, s4, :]
                p.dwait("vector", "xl")
                p.dwait("scalar", "xl")
                trstd, tnm = _ln_stats(p, x_sub, stats, junk)
                xn_half = xn_v[:, slot, :]
                p.wait("vector", "st", xn_slot_mm[slot])
                p.dwait("vector", "xs")
                txn = p.v("tensor_scalar", xn_half, x_sub, stats[:, 7:8],
                          stats[:, 8:9], ALU.mult, ALU.add)
                x_slot_ss[s4] = trstd
                p.v("reduce_max", stats[:, 9:10], x_sub, AX.X)
                p.v("tensor_reduce", stats[:, 10:11], x_sub, AX.X, ALU.min)
                p.v("tensor_scalar", stats[:, 11:12], stats[:, 9:10], stats[:, 2:3],
                    None, ALU.subtract)
                p.v("tensor_scalar", stats[:, 12:13], stats[:, 10:11], stats[:, 2:3],
                    -1.0, ALU.subtract, ALU.mult)
                p.v("tensor_tensor", stats[:, 13:14], stats[:, 11:12],
                    stats[:, 12:13], ALU.max)
                p.v("tensor_scalar", stats[:, 14:15], stats[:, 13:14], stats[:, 7:8],
                    None, ALU.mult)
                p.v("tensor_scalar", inv_sx[:, sub:sub + 1], stats[:, 14:15],
                    1e-5, 1.0 / 127.0, ALU.max, ALU.mult)
                p.v("reciprocal", stats[:, 15:16], inv_sx[:, sub:sub + 1])
                p.v("tensor_scalar", stats[:, 17:18], stats[:, 7:8], stats[:, 15:16],
                    None, ALU.mult)
                p.v("tensor_scalar", stats[:, 18:19], stats[:, 8:9], stats[:, 15:16],
                    None, ALU.mult)
                p.v("tensor_scalar", xitmp, x_sub, stats[:, 17:18],
                    stats[:, 18:19], ALU.mult, ALU.add)
                p.wait("vector", "st", t_xi_last)
                txi = p.v("tensor_scalar", xi_tm[:], xitmp, MAGIC, MAGIC,
                          ALU.add, ALU.subtract)
                x_slot_sv[s4] = txi
                p.wait("tensor", "sv", txi)
                if sub == 0:
                    p.wait("tensor", "sv", c["tok_idb"])
                for dc in range(8):
                    p.t("transpose", PS_T[:, dc * 128:(dc + 1) * 128],
                        xi_tm[:, dc * 128:(dc + 1) * 128], c["idb16"][:],
                        inc=(dc == 7))
                t_xi_last = p.cnt["st"]
                p.wait("vector", "st", t_xi_last)
                p.v("tensor_copy",
                    xi_fmv[:, :, sub * 128:(sub + 1) * 128],
                    PS_T[:].rearrange("p (c t) -> p c t", c=8))
                p.wait("tensor", "sv", txn)
                p.t("matmul", PS_A[0:1, 1536:2048], c["ones_c"][:],
                    xn_half[:, 0:512], start=(sub == 0), stop=(sub == NS - 1))
                t_colsum = p.t("matmul", PS_A[0:1, 1024:1536], c["ones_c"][:],
                               xn_half[:, 512:1024], start=(sub == 0),
                               stop=(sub == NS - 1), inc=True)
                xn_slot_mm[slot] = t_colsum
                p.wait("scalar", "sv", txn)
                p.dma("scalar", xnv[sub], xn_half, "xs")
        p.wait("vector", "st", t_colsum)
        p.v("tensor_copy", sum_sb[0:1, 0:512], PS_A[0:1, 1536:2048])
        tsum2 = p.v("tensor_copy", sum_sb[0:1, 512:1024], PS_A[0:1, 1024:1536])
        p.wait("sync", "sv", tsum2)
        p.dma("sync", xnsum, sum_sb[0:1, 0:DM], "xs")
        p.wait("vector", "sv", tmbc)
        p.v("tensor_scalar", sg_vec[:], inv_sx[:], mbc[:, 0:1], None, ALU.mult)
        p.v("tensor_scalar", su_vec[:], inv_sx[:], mbc[:, 1:2], None, ALU.mult)
        t_svec = p.cnt["sv"]

        sc_wgv = sc_wg.rearrange("(c p) o -> p c o", p=128)
        sc_wuv = sc_wu.rearrange("(c p) o -> p c o", p=128)
        wgq_v = wgq_sb[:].rearrange("p (c o) -> p c o", c=8)
        wuq_v = wuq_sb[:].rearrange("p (c o) -> p c o", c=8)
        guv = sc_gu.rearrange("(n p) o -> n p o", p=128)
        tok_gact = [0, 0]
        tok_uact = 0
        tok_gu = [0, 0]
        tok_gudma = [0, 0]
        tok_wq_used = 0
        for oq in range(4):
            p.wait("sync", "st", tok_wq_used)
            if oq == 0:
                p.dwait("sync", "ws")
            p.dma("sync", wgq_v, sc_wgv[:, :, oq * 1024:(oq + 1) * 1024], "wql")
            p.dma("sync", wuq_v, sc_wuv[:, :, oq * 1024:(oq + 1) * 1024], "wql")
            p.dwait("tensor", "wql")
            if oq == 0:
                p.wait("tensor", "sv", tsum2)
                p.wait("tensor", "sv", t_svec)
            for ts in range(NS):
                half = ts % 2
                gph = PS_A[:, half * 1024:half * 1024 + 1024]
                uph = PS_B[:, 0:1024]
                p.wait("tensor", "ss", tok_gact[half])
                t_gmm = None
                for k in range(8):
                    lhs = xi_fmv[:, k, ts * 128:(ts + 1) * 128]
                    p.t("matmul", gph[:, 0:512], lhs, wgq_v[:, k, 0:512],
                        start=(k == 0), stop=(k == 7))
                    t_gmm = p.t("matmul", gph[:, 512:1024], lhs,
                                wgq_v[:, k, 512:1024], start=(k == 0),
                                stop=(k == 7), inc=(k == 7))
                p.wait("tensor", "ss", tok_uact)
                t_umm = None
                for k in range(8):
                    lhs = xi_fmv[:, k, ts * 128:(ts + 1) * 128]
                    p.t("matmul", uph[:, 0:512], lhs, wuq_v[:, k, 0:512],
                        start=(k == 0), stop=(k == 7))
                    t_umm = p.t("matmul", uph[:, 512:1024], lhs,
                                wuq_v[:, k, 512:1024], start=(k == 0),
                                stop=(k == 7), inc=(k == 7))
                if ts == NS - 1:
                    tok_wq_used = t_umm
                gslot = ts % 2
                if oq == 0 and ts == 0:
                    p.wait("scalar", "sv", t_svec)
                p.wait("scalar", "st", t_gmm)
                p.wait("scalar", "sv", max(tok_gu))
                tok_gact[half] = p.s("activation", g_f32[:], gph, AF.Silu,
                                     bias=0.0, scale=sg_vec[:, ts:ts + 1])
                p.wait("scalar", "st", t_umm)
                tok_uact = p.s("activation", u_f32[:], uph, AF.Copy,
                               bias=0.0, scale=su_vec[:, ts:ts + 1])
                gu_half = gu_f32[:, gslot * 1024:(gslot + 1) * 1024]
                p.wait("vector", "ss", tok_uact)
                p.dwait("vector", "gs")
                tok_gu[gslot] = p.v("tensor_tensor", gu_half, g_f32[:], u_f32[:],
                                    ALU.mult)
                p.wait("gpsimd", "sv", tok_gu[gslot])
                p.dma("gpsimd", guv[ts][:, oq * 1024:(oq + 1) * 1024], gu_half,
                      "gs")

        sc_xi2v = sc_xi2.rearrange("(c p) t -> p c t", p=128)
        xi2sv = xi2_stg[:].rearrange("p (c t) -> p c t", c=32)
        tok_xi2dma = 0
        tok_guld_use = 0
        p.dwait("sync", "gs")
        for ts in range(NS):
            p.wait("sync", "sv", tok_guld_use)
            p.dma("sync", ld32[:], guv[ts], "gl")
            p.dwait("vector", "gl")
            p.v("reduce_max", stats[:, 9:10], ld32[:], AX.X,
                apply_absolute_value=True)
            p.v("tensor_scalar", stats[:, 10:11], stats[:, 9:10], 1e-5, 1.0 / 127.0,
                ALU.max, ALU.mult)
            p.v("tensor_scalar", sd_vec[:, ts:ts + 1], stats[:, 10:11], mbc[:, 2:3],
                None, ALU.mult)
            p.v("reciprocal", stats[:, 11:12], stats[:, 10:11])
            p.v("tensor_scalar", ld32[:], ld32[:], stats[:, 11:12], MAGIC,
                ALU.mult, ALU.add)
            p.wait("vector", "st", t_xi_last)
            tok_guld_use = p.v("tensor_scalar", xi2_tm[:], ld32[:], MAGIC, None,
                               ALU.subtract)
            p.wait("tensor", "sv", tok_guld_use)
            p.dwait("vector", "xst")
            tev = None
            for q4 in range(4):
                for dc in range(8):
                    p.t("transpose", PS_T[:, dc * 128:(dc + 1) * 128],
                        xi2_tm[:, (q4 * 8 + dc) * 128:(q4 * 8 + dc + 1) * 128],
                        c["idb16"][:], inc=(dc == 7))
                t_xi_last = p.cnt["st"]
                p.wait("vector", "st", t_xi_last)
                tev = p.v("tensor_copy", xi2sv[:, q4 * 8:(q4 + 1) * 8, :],
                          PS_T[:].rearrange("p (c t) -> p c t", c=8))
                if q4 < 3:
                    p.wait("tensor", "sv", tev)
            p.wait("scalar", "sv", tev)
            p.dma("scalar", sc_xi2v[:, :, ts * 128:(ts + 1) * 128], xi2sv, "xst")

        wdv = sc_wd.rearrange("(c p) d -> p c d", p=128)
        wd_v = wd_sb[:].rearrange("p (s c d) -> p s c d", s=2, c=4)
        xdv = xd_sb[:].rearrange("p (s c t) -> p s c t", s=2, c=32)
        fov = ffn_out.rearrange("(n p) d -> n p d", p=128)
        tok_fo_act = [0, 0]
        tok_fo_dma = [0, 0]
        tok_xd_use = [0, 0]
        tok_wd_use = [0, 0]
        tok_xd = [0, 0]
        tok_wd = [0, 0]
        p.dwait("sync", "xst")
        p.dma("sync", wd_v[:, 0], wdv[:, 0:4, :], "wd0")
        for pair in range(8):
            for m in range(2):
                ts = pair * 2 + m
                p.wait("sync", "st", tok_xd_use[m])
                p.dma("sync", xdv[:, m],
                      sc_xi2v[:, :, ts * 128:(ts + 1) * 128], "xd")
            pp = [PS_A[:, 0:1024], PS_A[:, 1024:2048]]
            p.dwait("tensor", "xd")
            t_pair_mm = None
            for kg in range(8):
                wslot = kg % 2
                if not (pair == 7 and kg == 7):
                    nkg = (kg + 1) % 8
                    nslot = nkg % 2
                    p.wait("sync", "st", tok_wd_use[nslot])
                    p.dma("sync", wd_v[:, nslot],
                          wdv[:, nkg * 4:(nkg + 1) * 4, :], "wd%d" % nslot)
                p.dwait("tensor", "wd%d" % wslot)
                if kg == 0:
                    p.wait("tensor", "ss", tok_fo_act[0])
                    p.wait("tensor", "ss", tok_fo_act[1])
                for kk in range(4):
                    k = kg * 4 + kk
                    wd_c = wd_v[:, wslot, kk, :]
                    for m in range(2):
                        lhs = xdv[:, m, k, :]
                        p.t("matmul", pp[m][:, 0:512], lhs, wd_c[:, 0:512],
                            start=(k == 0), stop=(k == 31))
                        tmm = p.t("matmul", pp[m][:, 512:1024], lhs,
                                  wd_c[:, 512:1024], start=(k == 0),
                                  stop=(k == 31),
                                  inc=(kk == 3 and m == 1))
                        if kk == 3 and m == 1:
                            tok_wd_use[wslot] = tmm
                            if k == 31:
                                t_pair_mm = tmm
            tok_xd_use = [t_pair_mm, t_pair_mm]
            for m in range(2):
                ts = pair * 2 + m
                p.wait("scalar", "st", t_pair_mm)
                p.dwait("scalar", "fs")
                tok_fo_act[m] = p.s("activation", fo_sb[:, m * 1024:(m + 1) * 1024],
                                    pp[m], AF.Copy, bias=0.0,
                                    scale=sd_vec[:, ts:ts + 1])
                p.wait("gpsimd", "ss", tok_fo_act[m])
                p.dma("gpsimd", fov[ts],
                      fo_sb[:, m * 1024:(m + 1) * 1024], "fs")

        p.drain_all()
        p.finish(block)
    return nc


_BUILT = {}


def _get(name, fn):
    if name not in _BUILT:
        _BUILT[name] = fn()
    return _BUILT[name]


def _c(a, dt=np.float32):
    return np.ascontiguousarray(np.asarray(a, dtype=dt))


def build_B():
    nc = bass.Bass("TRN2", target_bir_lowering=False)
    gi = lambda n, s: nc.dram_tensor(n, s, F32, kind="ExternalInput").ap()
    x_own = gi("x_own", [TOK, DM])
    xn_own = gi("xn_own", [TOK, DM])
    ffn0 = gi("ffn0", [TOK, DM])
    xnsumT = gi("xnsumT", [DM, 4])
    wvT = gi("wvT", [DM, DM])
    wfcT = gi("wfcT", [DM, DM])
    bfc_r = gi("bfc_r", [1, DM])
    gam_r = gi("gam_r", [1, DM])
    bet_r = gi("bet_r", [1, DM])
    W_in = gi("W_in", [DM, 2 * DIN])
    b_in_col = gi("b_in_col", [128, 32])
    W_xp = gi("W_xp", [DIN, 32])
    b_xp_col = gi("b_xp_col", [32, 1])
    W_dt = gi("W_dt", [DIN, DIN])
    b_dt_col = gi("b_dt_col", [128, 16])
    W_base = gi("W_base", [DM, DM])
    b_base_col = gi("b_base_col", [128, 8])
    Wd1 = gi("Wd1", [8, DM, 256])
    bd1_col = gi("bd1_col", [128, 16])
    Wd2 = gi("Wd2", [8, 256, DM])
    bd2_col = gi("bd2_col", [128, 64])
    go = lambda n, s, dt=F32: nc.dram_tensor(n, s, dt, kind="ExternalOutput").ap()
    x1_o = go("x1", [TOK, DM])
    dt_o = go("dt_s", [DIN, TOK])
    u_o = go("u_out", [DIN, TOK])
    sz_o = go("sz", [DIN, TOK])
    base_o = go("base", [DM, TOK])
    bc_o = go("bc", [32, TOK])
    xn1f_o = go("xn1fm", [DM, TOK], BF16)
    loss_o = go("losses", [8, 1])

    with ExitStack() as ctx:
        ec = ctx.enter_context
        p = Prog(nc, ctx)
        c = _mk_consts(p, nc, ctx, need_idb16=False)
        NS = 16

        stats = ec(nc.sbuf_tensor("stats", [128, 24], F32))
        xs_sb = ec(nc.sbuf_tensor("xs_sb", [128, 8 * 4], F32))
        xs_col = ec(nc.sbuf_tensor("xs_col", [128, 8], F32))
        csv_col = ec(nc.sbuf_tensor("csv_col", [128, 8], F32))
        wv_sb = ec(nc.sbuf_tensor("wv_sb", [128, 2 * 128], F32))
        wfc_sb = ec(nc.sbuf_tensor("wfc_sb", [128, 2 * 1024], F32))
        y2bc = ec(nc.sbuf_tensor("y2bc", [128, DM], F32))
        gambc = ec(nc.sbuf_tensor("gambc", [128, DM], F32))
        betbc = ec(nc.sbuf_tensor("betbc", [128, DM], F32))
        ev_f32 = ec(nc.sbuf_tensor("ev_f32", [128, 2 * TOK], F32))
        row_sb = ev_f32[0:1, 0:4 * DM]
        x_sb = ec(nc.sbuf_tensor("x_sb", [128, 2 * DM], F32))
        xn_sb = ec(nc.sbuf_tensor("xn_sb", [128, 2 * DM], F32))
        fo_sb = ec(nc.sbuf_tensor("fo_sb", [128, 2 * DM], F32))
        h_sb = ec(nc.sbuf_tensor("h_sb", [128, DM], F32))
        x1_sb = ec(nc.sbuf_tensor("x1_sb", [128, 2 * DM], F32))
        xn1_sb = ec(nc.sbuf_tensor("xn1_sb", [128, DM], F32))
        xn1_fm = ec(nc.sbuf_tensor("xn1_fm", [128, 8 * TOK], BF16))
        xcb = ec(nc.sbuf_tensor("xcb", [128, 16 * TOK], BF16))
        wst = ec(nc.sbuf_tensor("wst", [128, 2 * 16 * 128], BF16))
        heb = ec(nc.sbuf_tensor("heb", [128, 2 * TOK], BF16))
        diff = ec(nc.sbuf_tensor("diff", [128, TOK], F32))
        lacc = ec(nc.sbuf_tensor("lacc", [128, 8], F32))
        bias_sb = ec(nc.sbuf_tensor("bias_sb", [128, 144], F32))
        l_sb = ec(nc.sbuf_tensor("l_sb", [8, 1], F32))

        PS_A = ec(nc.psum_tensor("PS_A", [128, 2048], F32))
        PS_B = ec(nc.psum_tensor("PS_B", [128, 2048], F32))

        block = ec(nc.Block())
        xsv = xnsumT.rearrange("(c p) g -> p c g", p=128)

        p.dma("sync", bias_sb[:, 0:32], b_in_col, "b1")
        p.dma("sync", bias_sb[:, 32:48], b_dt_col, "b1")
        p.dma("sync", bias_sb[:, 48:56], b_base_col, "b1")
        p.dma("sync", bias_sb[:, 56:72], bd1_col, "b1")
        p.dma("sync", bias_sb[:, 72:136], bd2_col, "b1")
        p.dma("sync", bias_sb[0:32, 136:137], b_xp_col, "b1")
        p.dma("sync", xs_sb[:].rearrange("p (c g) -> p c g", g=4), xsv, "b1")
        p.dma("sync", row_sb[0:1, DM:2 * DM], bfc_r, "b1")
        p.dma("sync", row_sb[0:1, 2 * DM:3 * DM], gam_r, "b1")
        p.dma("sync", row_sb[0:1, 3 * DM:4 * DM], bet_r, "b1")
        p.dwait("vector", "b1")
        xsv2 = xs_sb[:].rearrange("p (c g) -> p c g", g=4)
        p.v("tensor_tensor", xs_col[:], xsv2[:, :, 0], xsv2[:, :, 1], ALU.add)
        p.v("tensor_tensor", stats[:, 0:8], xsv2[:, :, 2], xsv2[:, :, 3], ALU.add)
        t_xs = p.v("tensor_tensor", xs_col[:], xs_col[:], stats[:, 0:8], ALU.add)
        p.wait("tensor", "sv", t_xs)
        t_wv_used = 0
        for m in range(8):
            for k in range(8):
                slot = (m * 8 + k) % 2
                p.wait("sync", "st", t_wv_used)
                p.dma("sync", wv_sb[:, slot * 128:(slot + 1) * 128],
                      wvT[k * 128:(k + 1) * 128, m * 128:(m + 1) * 128], "wv%d" % slot)
                p.dwait("tensor", "wv%d" % slot)
                t_wv_used = p.t("matmul", PS_A[:, m:m + 1],
                                wv_sb[:, slot * 128:(slot + 1) * 128],
                                xs_col[:, k:k + 1], start=(k == 0), stop=(k == 7),
                                inc=True)
        p.wait("vector", "st", t_wv_used)
        t_csv = p.v("tensor_copy", csv_col[:], PS_A[:, 0:8])
        p.wait("tensor", "sv", t_csv)
        t_fc_used = 0
        for k in range(8):
            slot = k % 2
            p.wait("sync", "st", t_fc_used)
            p.dma("sync", wfc_sb[:, slot * 1024:(slot + 1) * 1024],
                  wfcT[k * 128:(k + 1) * 128, :], "wf%d" % slot)
            p.dwait("tensor", "wf%d" % slot)
            p.t("matmul", PS_B[0:1, 0:512], csv_col[:, k:k + 1],
                wfc_sb[:, slot * 1024:slot * 1024 + 512],
                start=(k == 0), stop=(k == 7))
            t_fc_used = p.t("matmul", PS_B[0:1, 512:1024], csv_col[:, k:k + 1],
                            wfc_sb[:, slot * 1024 + 512:slot * 1024 + 1024],
                            start=(k == 0), stop=(k == 7), inc=True)
        p.wait("vector", "st", t_fc_used)
        t_y2r = p.v("tensor_tensor", row_sb[0:1, 0:DM], PS_B[0:1, 0:DM],
                    row_sb[0:1, DM:2 * DM], ALU.add)
        p.wait("tensor", "sv", t_y2r)
        tb = None
        t_bc_prev = 0
        for (dst, off) in ((y2bc, 0), (gambc, 2 * DM), (betbc, 3 * DM)):
            p.wait("tensor", "sv", t_bc_prev)
            p.t("matmul", PS_A[:, 0:512], c["ones_r"][:],
                row_sb[0:1, off:off + 512], start=True, stop=True)
            p.t("matmul", PS_A[:, 512:1024], c["ones_r"][:],
                row_sb[0:1, off + 512:off + 1024], start=True, stop=True, inc=True)
            tb = p.cnt["st"]
            p.wait("vector", "st", tb)
            t_bc_prev = p.v("tensor_copy", dst[:], PS_A[:, 0:1024])
        t_bcasts = p.cnt["sv"]

        xv = x_own.rearrange("(n p) d -> n p d", p=128)
        xnv = xn_own.rearrange("(n p) d -> n p d", p=128)
        fov = ffn0.rearrange("(n p) d -> n p d", p=128)
        x1v = x1_o.rearrange("(n p) d -> n p d", p=128)
        xn1_fmv = xn1_fm[:].rearrange("p (c t) -> p c t", c=8)
        tok_ld_use = [(0, 0), (0, 0)]
        t_tp_last = 0
        for sub in range(NS):
            par = sub % 2
            sl = slice(par * DM, (par + 1) * DM)
            p.wait("sync", "sv", tok_ld_use[par][0])
            p.wait("sync", "ss", tok_ld_use[par][1])
            p.dma("sync", x_sb[:, sl], xv[sub], "xl%d" % par)
            p.dma("sync", xn_sb[:, sl], xnv[sub], "xl%d" % par)
            p.dma("sync", fo_sb[:, sl], fov[sub], "xl%d" % par)
            p.dwait("vector", "xl%d" % par)
            if sub == 0:
                p.wait("vector", "sv", t_bcasts)
            th = p.v("tensor_tensor", h_sb[:], xn_sb[:, sl], y2bc[:], ALU.add)
            trstd, tnm = _ln_stats(p, h_sb[:], stats, diff[:, 0:DM], sv_tok=th)
            p.v("tensor_scalar", h_sb[:], h_sb[:], stats[:, 7:8], stats[:, 8:9],
                ALU.mult, ALU.add)
            p.v("tensor_tensor", h_sb[:], h_sb[:], gambc[:], ALU.mult)
            p.v("tensor_tensor", h_sb[:], h_sb[:], betbc[:], ALU.add)
            p.v("tensor_tensor", h_sb[:], h_sb[:], x_sb[:, sl], ALU.add)
            p.dwait("vector", "x1s")
            tx1 = p.v("tensor_tensor", x1_sb[:, sl], h_sb[:], fo_sb[:, sl], ALU.add)
            p.wait("scalar", "sv", tx1)
            p.dma("scalar", x1v[sub], x1_sb[:, sl], "x1s")
            trstd1, tnm1 = _ln_stats(p, x1_sb[:, sl], stats, diff[:, 0:DM],
                                     sv_tok=tx1)
            p.wait("vector", "st", t_tp_last)
            txn1 = p.v("tensor_scalar", xn1_sb[:], x1_sb[:, sl], stats[:, 7:8],
                       stats[:, 8:9], ALU.mult, ALU.add)
            tok_ld_use[par] = (max(tx1, txn1), trstd1)
            p.wait("tensor", "sv", txn1)
            if sub == 0:
                p.wait("tensor", "sv", c["tok_id"])
            for dc in range(8):
                p.t("transpose", PS_B[:, dc * 128:(dc + 1) * 128],
                    xn1_sb[:, dc * 128:(dc + 1) * 128], c["id32"][:],
                    inc=(dc == 7))
            t_tp_last = p.cnt["st"]
            p.wait("vector", "st", t_tp_last)
            p.v("tensor_copy", xn1_fmv[:, :, sub * 128:(sub + 1) * 128],
                PS_B[:, 0:1024].rearrange("p (c t) -> p c t", c=8))
        xn1fv = xn1f_o.rearrange("(c p) t -> p c t", p=128)
        p.wait("sync", "sv", p.cnt["sv"])
        p.dma("sync", xn1fv, xn1_fmv, "xno")

        wstv = wst[:].rearrange("p (s c t) -> p s c t", s=2, c=16)
        uo_v = u_o.rearrange("(c p) t -> c p t", p=128)
        sz_v = sz_o.rearrange("(c p) t -> c p t", p=128)
        dt_v = dt_o.rearrange("(c p) t -> c p t", p=128)
        base_v = base_o.rearrange("(c p) t -> c p t", p=128)
        xcbv = xcb[:].rearrange("p (c t) -> p c t", c=16)
        pslots = [PS_A, PS_B]
        tok_ev = [0, 0]
        tok_ev_eng = ["sv", "sv"]
        tok_wst_use = [0, 0]
        tok_evb = [0, 0]

        def win_mm(src_w, kchunks, m, wslot, rhs_fn, psl, extra_wait=None):
            p.wait("gpsimd", "st", tok_wst_use[wslot])
            p.dma("gpsimd", wstv[:, wslot, 0:kchunks, :], src_w, "wst%d" % wslot)
            p.dwait("tensor", "wst%d" % wslot)
            if extra_wait is not None:
                p.wait("tensor", extra_wait[0], extra_wait[1])
            tok = None
            for k in range(kchunks):
                rhs = rhs_fn(k)
                for q in range(4):
                    tok = p.t("matmul", psl[:, q * 512:(q + 1) * 512],
                              wstv[:, wslot, k, :], rhs[:, q * 512:(q + 1) * 512],
                              start=(k == 0), stop=(k == kchunks - 1),
                              inc=(k == kchunks - 1 and q == 3))
            return tok

        p.wait("tensor", "sv", p.cnt["sv"])
        for m in range(32):
            psl = pslots[m % 2][:, 0:2048]
            wslot = m % 2
            ew = ("sv" if tok_ev_eng[m % 2] == "sv" else "ss", tok_ev[m % 2])
            src = W_in.rearrange("k (mc mp) -> mc k mp", mp=128)[m]
            src = src.rearrange("(kc p) mp -> p kc mp", p=128)
            tok_mm = win_mm(src, 8, m, wslot, lambda k: xn1_fmv[:, k, :], psl)
            tok_wst_use[wslot] = tok_mm
            p.wait("scalar", "st", tok_mm)
            p.dwait("scalar", "evb%d" % (m % 2))
            if m < 16:
                tev = p.s("activation", ev_f32[:, (m % 2) * TOK:(m % 2 + 1) * TOK],
                          psl, AF.Silu, bias=bias_sb[:, m:m + 1], scale=1.0)
                p.wait("vector", "ss", tev)
                tcb = p.v("tensor_copy", xcbv[:, m, :],
                          ev_f32[:, (m % 2) * TOK:(m % 2 + 1) * TOK])
                p.wait("gpsimd", "sv", tcb)
                p.dma("gpsimd", uo_v[m], ev_f32[:, (m % 2) * TOK:(m % 2 + 1) * TOK],
                      "evb%d" % (m % 2))
            else:
                tev = p.s("activation", ev_f32[:, (m % 2) * TOK:(m % 2 + 1) * TOK],
                          psl, AF.Silu, bias=bias_sb[:, m:m + 1], scale=1.0)
                p.wait("gpsimd", "ss", tev)
                p.dma("gpsimd", sz_v[m - 16],
                      ev_f32[:, (m % 2) * TOK:(m % 2 + 1) * TOK], "evb%d" % (m % 2))
            tok_ev[m % 2] = tev
            tok_ev_eng[m % 2] = "ss"
        for m in range(16):
            psl = pslots[m % 2][:, 0:2048]
            wslot = m % 2
            src = W_dt.rearrange("k (mc mp) -> mc k mp", mp=128)[m]
            src = src.rearrange("(kc p) mp -> p kc mp", p=128)
            ew = (tok_ev_eng[m % 2], tok_ev[m % 2])
            p.wait("tensor", ew[0], ew[1])
            tok_mm = win_mm(src, 16, m, wslot, lambda k: xcbv[:, k, :], psl)
            tok_wst_use[wslot] = tok_mm
            p.wait("vector", "st", tok_mm)
            p.dwait("vector", "evb%d" % (m % 2))
            if m == 0:
                p.dwait("scalar", "x1s")
            ev = ev_f32[:, (m % 2) * TOK:(m % 2 + 1) * TOK]
            tmp2 = x1_sb[:, 0:2 * DM]
            p.v("tensor_scalar", diff[:], psl, bias_sb[:, 32 + m:33 + m], None, ALU.add)
            tab = p.cnt["sv"]
            p.wait("scalar", "sv", tab)
            t_abs = p.s("activation", tmp2, diff[:], AF.Abs, bias=0.0, scale=1.0)
            p.s("activation", ev, tmp2, AF.Exp, bias=0.0, scale=-1.0)
            te2 = p.s("activation", tmp2, ev, AF.Ln, bias=1.0, scale=1.0)
            p.wait("vector", "ss", t_abs)
            p.v("tensor_scalar", diff[:], diff[:], 0.0, None, ALU.max)
            p.wait("vector", "ss", te2)
            tsp = p.v("tensor_tensor", ev, tmp2, diff[:], ALU.add)
            p.wait("gpsimd", "sv", tsp)
            p.dma("gpsimd", dt_v[m], ev, "evb%d" % (m % 2))
            tok_ev[m % 2] = tsp
            tok_ev_eng[m % 2] = "sv"
        wxp_sb = wst[:].rearrange("p (s c t) -> p s c t", s=2, c=16)
        p.wait("gpsimd", "st", tok_wst_use[0])
        p.dma("gpsimd", wstv[:, 0, 0:16, 0:32],
              W_xp.rearrange("(kc p) o -> p kc o", p=128), "wst0")
        p.dwait("tensor", "wst0")
        p.wait("tensor", tok_ev_eng[0], tok_ev[0])
        tok = None
        for k in range(16):
            for q in range(4):
                tok = p.t("matmul", PS_A[0:32, q * 512:(q + 1) * 512],
                          wstv[:, 0, k, 0:32], xcbv[:, k, q * 512:(q + 1) * 512],
                          start=(k == 0), stop=(k == 15),
                          inc=(k == 15 and q == 3))
        tok_wst_use[0] = tok
        p.wait("vector", "st", tok)
        p.dwait("vector", "evb0")
        tbc = p.v("tensor_scalar", ev_f32[:, 0:TOK].rearrange("p t -> p t")[0:32, :],
                  PS_A[0:32, 0:2048], bias_sb[0:32, 136:137], None, ALU.add)
        p.wait("gpsimd", "sv", tbc)
        p.dma("gpsimd", bc_o, ev_f32[0:32, 0:TOK], "evb0")
        tok_ev[0] = tbc
        tok_ev_eng[0] = "sv"
        for m in range(8):
            psl = pslots[m % 2][:, 0:2048]
            wslot = m % 2
            src = W_base.rearrange("k (mc mp) -> mc k mp", mp=128)[m]
            src = src.rearrange("(kc p) mp -> p kc mp", p=128)
            ew = (tok_ev_eng[m % 2], tok_ev[m % 2])
            p.wait("tensor", ew[0], ew[1])
            tok_mm = win_mm(src, 8, m, wslot, lambda k: xn1_fmv[:, k, :], psl)
            tok_wst_use[wslot] = tok_mm
            p.wait("vector", "st", tok_mm)
            p.dwait("vector", "evb%d" % (m % 2))
            ev = ev_f32[:, (m % 2) * TOK:(m % 2 + 1) * TOK]
            tb = p.v("tensor_scalar", ev, psl, bias_sb[:, 48 + m:49 + m], None, ALU.add)
            p.wait("gpsimd", "sv", tb)
            p.dma("gpsimd", base_v[m], ev, "evb%d" % (m % 2))
            tok_ev[m % 2] = tb
            tok_ev_eng[m % 2] = "sv"
        p.v("memset", lacc[:], 0.0)
        t_heb_use = 0
        for e in range(8):
            for m in range(2):
                psl = pslots[m][:, 0:2048]
                ew = (tok_ev_eng[m], tok_ev[m])
                p.wait("tensor", ew[0], ew[1])
                src = Wd1[e].rearrange("k (mc mp) -> mc k mp", mp=128)[m]
                src = src.rearrange("(kc p) mp -> p kc mp", p=128)
                tok_mm = win_mm(src, 8, e * 2 + m, m, lambda k: xn1_fmv[:, k, :], psl)
                tok_wst_use[m] = tok_mm
                p.wait("scalar", "st", max(tok_mm, t_heb_use))
                the = p.s("activation", heb[:, m * TOK:(m + 1) * TOK], psl, AF.Relu,
                          bias=bias_sb[:, 56 + e * 2 + m:57 + e * 2 + m], scale=1.0)
                tok_ev[m] = the
                tok_ev_eng[m] = "ss"
            t_heb = p.cnt["ss"]
            hebv = heb[:].rearrange("p (c t) -> p c t", c=2)
            for m in range(8):
                psl = pslots[m % 2][:, 0:2048]
                ew = (tok_ev_eng[m % 2], tok_ev[m % 2])
                p.wait("tensor", ew[0], ew[1])
                src = Wd2[e].rearrange("k (mc mp) -> mc k mp", mp=128)[m]
                src = src.rearrange("(kc p) mp -> p kc mp", p=128)
                p.wait("gpsimd", "st", tok_wst_use[m % 2])
                p.dma("gpsimd", wstv[:, m % 2, 0:2, :], src, "wst%d" % (m % 2))
                p.dwait("tensor", "wst%d" % (m % 2))
                if m == 0:
                    p.wait("tensor", "ss", t_heb)
                tok = None
                for k in range(2):
                    for q in range(4):
                        tok = p.t("matmul", psl[:, q * 512:(q + 1) * 512],
                                  wstv[:, m % 2, k, :],
                                  hebv[:, k, q * 512:(q + 1) * 512],
                                  start=(k == 0), stop=(k == 1),
                                  inc=(k == 1 and q == 3))
                tok_wst_use[m % 2] = tok
                t_heb_use = tok
                p.wait("vector", "st", tok)
                td = p.v("scalar_tensor_tensor", diff[:], psl,
                         bias_sb[:, 72 + e * 8 + m:73 + e * 8 + m], xn1_fmv[:, m, :],
                         ALU.add, ALU.subtract)
                p.wait("scalar", "sv", td)
                p.dwait("scalar", "evb0")
                tsq = p.s("activation", ev_f32[:, 0:TOK], diff[:], AF.Square,
                          bias=0.0, scale=1.0, accum_out=stats[:, 20:21])
                p.wait("vector", "ss", tsq)
                tla = p.v("tensor_tensor", lacc[:, e:e + 1], lacc[:, e:e + 1],
                          stats[:, 20:21], ALU.add)
                tok_ev[m % 2] = max(tsq, 0)
                tok_ev_eng[m % 2] = "ss"
        p.wait("tensor", "sv", p.cnt["sv"])
        tml = p.t("matmul", PS_A[0:8, 0:1], lacc[:], c["ones_c"][:],
                  start=True, stop=True, inc=True)
        p.wait("vector", "st", tml)
        tls = p.v("tensor_copy", l_sb[:], PS_A[0:8, 0:1])
        p.wait("sync", "sv", tls)
        p.dma("sync", loss_o, l_sb[:], "xno")

        p.drain_all()
        p.finish(block)
    return nc


def build_C():
    nc = bass.Bass("TRN2", target_bir_lowering=False)
    dt_sh = nc.dram_tensor("dt_sh", [512, BT], F32, kind="ExternalInput")
    u_sh = nc.dram_tensor("u_sh", [512, BT], F32, kind="ExternalInput")
    bc_full = nc.dram_tensor("bc_full", [32, BT], F32, kind="ExternalInput")
    A_sl = nc.dram_tensor("A_sl", [512, DST], F32, kind="ExternalInput").ap()
    D_sl = nc.dram_tensor("D_sl", [128, 4], F32, kind="ExternalInput").ap()
    Y_sh = nc.dram_tensor("Y_sh", [512, BT], F32, kind="ExternalOutput").ap()
    TC = 1024
    NCH = BT // TC

    with ExitStack() as ctx:
        ec = ctx.enter_context
        p = Prog(nc, ctx)

        dt_ld = ec(nc.sbuf_tensor("dt_ld", [128, 2 * 4 * TC], F32))
        u_ld = ec(nc.sbuf_tensor("u_ld", [128, 2 * 4 * TC], F32))
        w_sb = ec(nc.sbuf_tensor("w_sb", [128, 4 * TC], F32))
        y_sb = ec(nc.sbuf_tensor("y_sb", [128, 4 * TC], F32))
        bcb = ec(nc.sbuf_tensor("bcb", [128, 4 * TC], F32))
        dA = ec(nc.sbuf_tensor("dA", [128, 2 * TC], F32))
        dbu = ec(nc.sbuf_tensor("dbu", [128, TC], F32))
        hout = ec(nc.sbuf_tensor("hout", [128, TC], F32))
        pbuf = ec(nc.sbuf_tensor("pbuf", [128, TC], F32))
        A_sb = ec(nc.sbuf_tensor("A_sb", [128, 4 * DST], F32))
        At = ec(nc.sbuf_tensor("At", [128, 4 * DST], F32))
        D_sb = ec(nc.sbuf_tensor("D_sb", [128, 4], F32))
        hlast = ec(nc.sbuf_tensor("hlast", [128, 64], F32))

        block = ec(nc.Block())

        dtv = dt_sh.ap().rearrange("(g p) t -> p g t", p=128)
        uv = u_sh.ap().rearrange("(g p) t -> p g t", p=128)
        yv = Y_sh.rearrange("(g p) t -> p g t", p=128)
        dt_lv = dt_ld[:].rearrange("p (r g t) -> p r g t", r=2, g=4)
        u_lv = u_ld[:].rearrange("p (r g t) -> p r g t", r=2, g=4)
        w_v = w_sb[:].rearrange("p (g t) -> p g t", g=4)
        y_v = y_sb[:].rearrange("p (g t) -> p g t", g=4)
        bcv = bcb[:].rearrange("p (k t) -> p k t", k=4)

        p.dma("sync", A_sb[:].rearrange("p (g s) -> p g s", g=4),
              A_sl.rearrange("(g p) s -> p g s", p=128), "cst")
        p.dma("sync", D_sb[:], D_sl, "cst")
        p.dwait("scalar", "cst")
        p.s("activation", At[:], A_sb[:], AF.Exp, bias=0.0, scale=1.0)
        ta = p.cnt["ss"]
        p.wait("vector", "ss", ta)
        tA = p.v("tensor_scalar", A_sb[:], At[:], -1.0, None, ALU.mult)
        tz = p.v("memset", hlast[:], 0.0)
        p.wait("scalar", "sv", tA)

        tok_w = 0
        tok_dAr = [0, 0]
        tok_bc_use = [0, 0]
        t_ydma = 0
        for ct in range(NCH):
            par = ct % 2
            tsl = slice(ct * TC, (ct + 1) * TC)
            p.wait("sync", "sv", tok_w if ct >= 2 else 0)
            p.dma("sync", dt_lv[:, par], dtv[:, :, tsl], "dt%d" % par)
            p.dma("sync", u_lv[:, par], uv[:, :, tsl], "u%d" % par)
            p.dwait("vector", "dt%d" % par)
            p.dwait("scalar", "dt%d" % par)
            p.dwait("vector", "u%d" % par)
            tok_w = 0
            for g in range(4):
                tok_w = p.v("tensor_tensor", w_v[:, g, :], dt_lv[:, par, g, :],
                            u_lv[:, par, g, :], ALU.mult)
            for s in range(DST):
                bpar = s % 2
                p.wait("sync", "sv", tok_bc_use[bpar])
                p.dma("sync", bcv[:, bpar, :],
                      bass.AP(bc_full, s * BT + ct * TC, [[0, 128], [1, TC]]),
                      "bc%d" % bpar)
                p.dma("sync", bcv[:, 2 + bpar, :],
                      bass.AP(bc_full, (16 + s) * BT + ct * TC, [[0, 128], [1, TC]]),
                      "bc%d" % bpar)
                p.dwait("vector", "bc%d" % bpar)
                for g in range(4):
                    gs = s * 4 + g
                    apar = gs % 2
                    dA_sl = dA[:, apar * TC:(apar + 1) * TC]
                    p.wait("scalar", "sv", tok_dAr[apar])
                    if gs == 0 and ct == 0:
                        pass
                    ta = p.s("activation", dA_sl, dt_lv[:, par, g, :], AF.Exp,
                             bias=0.0, scale=A_sb[:, g * DST + s:g * DST + s + 1])
                    tdbu = p.v("tensor_tensor", dbu[:], w_v[:, g, :],
                               bcv[:, bpar, :], ALU.mult)
                    p.wait("vector", "ss", ta)
                    col = g * DST + s
                    tok_dAr[apar] = p.v("tensor_tensor_scan", hout[:], dA_sl,
                                        dbu[:], hlast[:, col:col + 1],
                                        ALU.mult, ALU.add)
                    p.v("tensor_copy", hlast[:, col:col + 1], hout[:, TC - 1:TC])
                    tpm = p.v("tensor_tensor", pbuf[:], hout[:],
                              bcv[:, 2 + bpar, :], ALU.mult)
                    if s == 0:
                        p.dwait("vector", "ys")
                        p.v("scalar_tensor_tensor", y_v[:, g, :],
                            u_lv[:, par, g, :], D_sb[:, g:g + 1], pbuf[:],
                            ALU.mult, ALU.add)
                    else:
                        p.v("tensor_tensor", y_v[:, g, :], y_v[:, g, :],
                            pbuf[:], ALU.add)
                    if g == 3:
                        tok_bc_use[bpar] = p.cnt["sv"]
            p.wait("sync", "sv", p.cnt["sv"])
            t_ydma = p.dma("sync", yv[:, :, tsl], y_v, "ys")

        p.drain_all()
        p.finish(block)
    return nc


def build_D():
    nc = bass.Bass("TRN2", target_bir_lowering=False)
    gi = lambda n, s: nc.dram_tensor(n, s, F32, kind="ExternalInput").ap()
    x1fm = gi("x1fm", [DM, TOK])
    Y_own = gi("Y_own", [DIN, TOK])
    sz = gi("sz", [DIN, TOK])
    base = gi("base", [DM, TOK])
    xn1fm = nc.dram_tensor("xn1fm", [DM, TOK], BF16, kind="ExternalInput").ap()
    lflat = gi("lflat", [1, 32])
    W_out = gi("W_out", [DIN, DM])
    b_out_col = gi("b_out_col", [128, 8])
    Wa_down = gi("Wa_down", [8, DM, 64])
    ba_down = gi("ba_down", [8, 64])
    Wa_up = gi("Wa_up", [8, 64, DM])
    ba_up = gi("ba_up", [8, DM])
    x2fm = nc.dram_tensor("x2fm", [DM, TOK], F32, kind="ExternalOutput").ap()

    with ExitStack() as ctx:
        ec = ctx.enter_context
        p = Prog(nc, ctx)
        c = _mk_consts(p, nc, ctx, need_idb16=False)

        xn1_sb = ec(nc.sbuf_tensor("xn1_sb", [128, 8 * TOK], BF16))
        y2_sb = ec(nc.sbuf_tensor("y2_sb", [128, 16 * TOK], BF16))
        yld = ec(nc.sbuf_tensor("yld", [128, 2 * TOK], F32))
        szld = ec(nc.sbuf_tensor("szld", [128, 2 * TOK], F32))
        x1ld = ec(nc.sbuf_tensor("x1ld", [128, 2 * TOK], F32))
        bsld = ec(nc.sbuf_tensor("bsld", [128, 2 * TOK], F32))
        out_sb = ec(nc.sbuf_tensor("out_sb", [128, 2 * TOK], F32))
        wo_sb = ec(nc.sbuf_tensor("wo_sb", [128, 2 * 16 * 128], BF16))
        small = ec(nc.sbuf_tensor("small", [128, 64], F32))
        wa_ld = ec(nc.sbuf_tensor("wa_ld", [128, 8 * 64], F32))
        acc64 = ec(nc.sbuf_tensor("acc64", [128, 128], F32))
        wadn = ec(nc.sbuf_tensor("wadn", [128, 8 * 64], BF16))
        wau_ld = ec(nc.sbuf_tensor("wau_ld", [64, 8 * 128], F32))
        waup = ec(nc.sbuf_tensor("waup", [64, 8 * 128], BF16))
        brow = ec(nc.sbuf_tensor("brow", [8, DM], F32))
        arow = ec(nc.sbuf_tensor("arow", [1, DM], F32))
        a_b16 = ec(nc.sbuf_tensor("a_b16", [64, TOK], BF16))

        PS_A = ec(nc.psum_tensor("PS_A", [128, 2048], F32))
        PS_B = ec(nc.psum_tensor("PS_B", [128, 2048], F32))

        block = ec(nc.Block())

        p.dma("sync", small[0:1, 0:32], lflat, "cst")
        p.dwait("vector", "cst")
        lv = small[0:1, 0:32].rearrange("p (g e) -> p g e", g=4)
        p.v("tensor_tensor", small[0:1, 32:40], lv[:, 0, :], lv[:, 1, :], ALU.add)
        p.v("tensor_tensor", small[0:1, 40:48], lv[:, 2, :], lv[:, 3, :], ALU.add)
        p.v("tensor_tensor", small[0:1, 32:40], small[0:1, 32:40],
            small[0:1, 40:48], ALU.add)
        p.v("tensor_reduce", small[0:1, 48:49], small[0:1, 32:40], AX.X, ALU.min)
        toh = p.v("tensor_scalar", small[0:1, 56:64], small[0:1, 32:40],
                  small[0:1, 48:49], None, ALU.is_equal)
        p.wait("tensor", "sv", toh)
        tmk = p.t("matmul", PS_A[:, 0:8], c["ones_r"][:], small[0:1, 56:64],
                  start=True, stop=True, inc=True)
        p.wait("vector", "st", tmk)
        tmbc = p.v("tensor_copy", small[:, 0:8], PS_A[:, 0:8])
        p.t("transpose", PS_A[0:8, 8:9], small[0:1, 56:64],
            c["ones_r"][0:1, 0:1], inc=True)
        p.wait("vector", "st", p.cnt["st"])
        tohc = p.v("tensor_copy", small[0:8, 9:10], PS_A[0:8, 8:9])

        wadn_v = wadn[:].rearrange("p (k r) -> p k r", k=8)
        waup_v = waup[:].rearrange("p (m t) -> p m t", m=8)
        wa_lv = wa_ld[:].rearrange("p (e r) -> p e r", e=8)
        t_sel = tmbc
        for kc in range(8):
            p.wait("sync", "sv", t_sel if kc else 0)
            p.dma("sync", wa_lv,
                  Wa_down.rearrange("e (kc p) r -> kc p e r", p=128)[kc], "wa")
            p.dwait("vector", "wa")
            p.wait("vector", "sv", tmbc)
            p.v("memset", acc64[:, 0:64], 0.0)
            for e in range(8):
                p.v("scalar_tensor_tensor", acc64[:, 0:64],
                    wa_lv[:, e, :], small[:, e:e + 1], acc64[:, 0:64],
                    ALU.mult, ALU.add)
            t_sel = p.v("tensor_copy", wadn_v[:, kc, :], acc64[:, 0:64])
        wau_lv = wau_ld[:].rearrange("p (e t) -> p e t", e=8)
        for mc in range(8):
            p.wait("sync", "sv", t_sel)
            p.dma("sync", wau_lv,
                  Wa_up.rearrange("e r (mc p) -> mc r e p", p=128)[mc], "wa")
            p.dwait("vector", "wa")
            p.v("memset", acc64[0:64, 0:128], 0.0)
            for e in range(8):
                p.v("scalar_tensor_tensor", acc64[0:64, 0:128], wau_lv[0:64, e, :],
                    small[0:64, e:e + 1], acc64[0:64, 0:128], ALU.mult, ALU.add)
            t_sel = p.v("tensor_copy", waup_v[0:64, mc, :], acc64[0:64, 0:128])
        p.dma("sync", brow[0:8, 0:64], ba_down, "cst")
        p.dwait("tensor", "cst")
        p.wait("tensor", "sv", tohc)
        t_mm_bd = p.t("matmul", PS_A[0:1, 16:80], small[0:8, 9:10],
                      brow[0:8, 0:64], start=True, stop=True, inc=True)
        p.wait("vector", "st", p.cnt["st"])
        tbd = p.v("tensor_copy", arow[0:1, 0:64], PS_A[0:1, 16:80])
        p.wait("tensor", "sv", tbd)
        p.t("transpose", PS_A[0:64, 80:81], arow[0:1, 0:64],
            c["ones_r"][0:1, 0:1], inc=True)
        p.wait("vector", "st", p.cnt["st"])
        tbdc = p.v("tensor_copy", small[0:64, 10:11], PS_A[0:64, 80:81])
        p.wait("sync", "st", t_mm_bd)
        p.dma("sync", brow[:], ba_up, "cst")
        p.dwait("tensor", "cst")
        p.t("matmul", PS_A[0:1, 512:1024], small[0:8, 9:10], brow[0:8, 0:512],
            start=True, stop=True)
        p.t("matmul", PS_A[0:1, 1024:1536], small[0:8, 9:10], brow[0:8, 512:1024],
            start=True, stop=True, inc=True)
        p.wait("vector", "st", p.cnt["st"])
        tbu = p.v("tensor_copy", arow[0:1, 0:512], PS_A[0:1, 512:1024])
        tbu = p.v("tensor_copy", arow[0:1, 512:1024], PS_A[0:1, 1024:1536])
        p.wait("tensor", "sv", tbu)
        for mc in range(8):
            p.t("transpose", PS_B[0:128, mc:mc + 1],
                arow[0:1, mc * 128:(mc + 1) * 128], c["ones_r"][0:1, 0:1],
                inc=(mc == 7))
        p.wait("vector", "st", p.cnt["st"])
        tbuc = p.v("tensor_copy", small[:, 40:48], PS_B[0:128, 0:8])

        Yv = Y_own.rearrange("(c p) t -> c p t", p=128)
        szv = sz.rearrange("(c p) t -> c p t", p=128)
        y2v = y2_sb[:].rearrange("p (c t) -> p c t", c=16)
        ty2 = 0
        for k in range(16):
            par = k % 2
            sl = slice(par * TOK, (par + 1) * TOK)
            p.wait("sync", "sv", ty2)
            p.dma("sync", yld[:, sl], Yv[k], "yl%d" % par)
            p.dma("sync", szld[:, sl], szv[k], "yl%d" % par)
            p.dwait("vector", "yl%d" % par)
            ty2 = p.v("tensor_tensor", y2v[:, k, :], yld[:, sl], szld[:, sl],
                      ALU.mult)
        p.dma("sync", xn1_sb[:].rearrange("p (c t) -> p c t", c=8),
              xn1fm.rearrange("(c p) t -> p c t", p=128), "xn1")

        xn1v = xn1_sb[:].rearrange("p (c t) -> p c t", c=8)
        p.dwait("tensor", "xn1")
        p.wait("tensor", "sv", max(t_sel, tbdc))
        ta_mm = None
        for k in range(8):
            for q in range(4):
                ta_mm = p.t("matmul", PS_A[0:64, q * 512:(q + 1) * 512],
                            wadn_v[:, k, :], xn1v[:, k, q * 512:(q + 1) * 512],
                            start=(k == 0), stop=(k == 7),
                            inc=(k == 7 and q == 3))
        p.wait("scalar", "st", ta_mm)
        tact = p.s("activation", a_b16[0:64, :], PS_A[0:64, 0:2048], AF.Relu,
                   bias=small[0:64, 10:11], scale=1.0)

        x1v = x1fm.rearrange("(c p) t -> c p t", p=128)
        basev = base.rearrange("(c p) t -> c p t", p=128)
        x2v = x2fm.rearrange("(c p) t -> c p t", p=128)
        wov = W_out.rearrange("k (mc p) -> mc k p", p=128)
        wo_v = wo_sb[:].rearrange("p (s k c) -> p s k c", s=2, k=16)
        tout = [0, 0]
        tod = [0, 0]
        t_wo_use = [0, 0]
        p.dma("sync", small[:, 48:56], b_out_col, "cst")
        p.dwait("vector", "cst")
        p.wait("vector", "sv", tbuc)
        tbb = p.v("tensor_tensor", small[:, 48:56], small[:, 48:56],
                  small[:, 40:48], ALU.add)
        for m in range(8):
            par = m % 2
            psl = [PS_A, PS_B][par][:, 0:2048]
            sl = slice(par * TOK, (par + 1) * TOK)
            p.wait("gpsimd", "st", t_wo_use[par])
            p.dma("gpsimd", wo_v[:, par],
                  wov[m].rearrange("(kc p) mp -> p kc mp", p=128), "wo%d" % par)
            p.dwait("tensor", "wo%d" % par)
            p.wait("tensor", "sv", tout[par])
            if m < 2:
                p.wait("tensor", "sv", ty2)
                p.wait("tensor", "ss", tact)
            tmm = None
            for k in range(16):
                for q in range(4):
                    tmm = p.t("matmul", psl[:, q * 512:(q + 1) * 512],
                              wo_v[:, par, k, :], y2v[:, k, q * 512:(q + 1) * 512],
                              start=(k == 0), stop=False)
            for q in range(4):
                tmm = p.t("matmul", psl[:, q * 512:(q + 1) * 512],
                          waup_v[0:64, m, :], a_b16[0:64, q * 512:(q + 1) * 512],
                          start=False, stop=True,
                          inc=(q == 3))
            t_wo_use[par] = tmm
            p.wait("sync", "sv", tout[par])
            p.dma("sync", x1ld[:, sl], x1v[m], "xb%d" % par)
            p.dma("sync", bsld[:, sl], basev[m], "xb%d" % par)
            p.wait("vector", "st", tmm)
            p.dwait("vector", "xb%d" % par)
            p.dwait("vector", "xo%d" % par)
            p.v("scalar_tensor_tensor", out_sb[:, sl], psl,
                small[:, 48 + m:49 + m], x1ld[:, sl], ALU.add, ALU.add)
            tadd = p.v("tensor_tensor", out_sb[:, sl], out_sb[:, sl],
                       bsld[:, sl], ALU.add)
            tout[par] = tadd
            p.wait("sync", "sv", tadd)
            p.dma("sync", x2v[m], out_sb[:, sl], "xo%d" % par)

        p.drain_all()
        p.finish(block)
    return nc


def _col(v, nchunk):
    return _c(np.ascontiguousarray(np.asarray(v, np.float32).reshape(nchunk, 128).T))


def kernel(L0_attn_x, L0_attn_w_qkv, L0_attn_w_fc, L0_attn_b_fc, L0_attn_gamma,
           L0_attn_beta, L0_ffn_w_gate, L0_ffn_w_up, L0_ffn_w_down,
           L1_attn_W_in, L1_attn_b_in, L1_attn_W_xproj, L1_attn_b_xproj,
           L1_attn_W_dt, L1_attn_b_dt, L1_attn_W_out, L1_attn_b_out,
           L1_attn_A_log, L1_attn_D_param, L1_ffn_W_base, L1_ffn_b_base,
           L1_ffn_Wd1, L1_ffn_bd1, L1_ffn_Wd2, L1_ffn_bd2, L1_ffn_Wa_down,
           L1_ffn_ba_down, L1_ffn_Wa_up, L1_ffn_ba_up):
    x = np.asarray(L0_attn_x, np.float32)
    cores = list(range(NC))

    wgT = _c(np.asarray(L0_ffn_w_gate, np.float32).T)
    wuT = _c(np.asarray(L0_ffn_w_up, np.float32).T)
    wdT = _c(np.asarray(L0_ffn_w_down, np.float32).T)
    inA = []
    for cid in cores:
        b, q = cid // 4, cid % 4
        inA.append(dict(x_own=_c(x[b, q * TOK:(q + 1) * TOK]),
                        wgT=wgT, wuT=wuT, wdT=wdT))
    rA = run_bass_kernel_spmd(_get("A", build_A), inA, core_ids=cores).results

    bd1c = _c(np.concatenate([np.asarray(L1_ffn_bd1, np.float32)[e].reshape(2, 128).T
                              for e in range(8)], axis=1))
    bd2c = _c(np.concatenate([np.asarray(L1_ffn_bd2, np.float32)[e].reshape(8, 128).T
                              for e in range(8)], axis=1))
    wvT = _c(np.asarray(L0_attn_w_qkv, np.float32)[2048:].T)
    wfcT = _c(np.asarray(L0_attn_w_fc, np.float32).T)
    shared_B = dict(
        wvT=wvT, wfcT=wfcT,
        bfc_r=_c(np.asarray(L0_attn_b_fc, np.float32)[None, :]),
        gam_r=_c(np.asarray(L0_attn_gamma, np.float32)[None, :]),
        bet_r=_c(np.asarray(L0_attn_beta, np.float32)[None, :]),
        W_in=_c(L1_attn_W_in), b_in_col=_col(L1_attn_b_in, 32),
        W_xp=_c(L1_attn_W_xproj),
        b_xp_col=_c(np.asarray(L1_attn_b_xproj, np.float32).reshape(32, 1)),
        W_dt=_c(L1_attn_W_dt), b_dt_col=_col(L1_attn_b_dt, 16),
        W_base=_c(L1_ffn_W_base), b_base_col=_col(L1_ffn_b_base, 8),
        Wd1=_c(L1_ffn_Wd1), bd1_col=bd1c, Wd2=_c(L1_ffn_Wd2), bd2_col=bd2c)
    inB = []
    for cid in cores:
        b, q = cid // 4, cid % 4
        parts = np.stack([np.asarray(rA[b * 4 + i]["xnsum"])[0] for i in range(4)],
                         axis=1)
        inB.append(dict(x_own=inA[cid]["x_own"],
                        xn_own=_c(rA[cid]["xn_out"]),
                        ffn0=_c(rA[cid]["ffn_out"]),
                        xnsumT=_c(parts), **shared_B))
    rB = run_bass_kernel_spmd(_get("B", build_B), inB, core_ids=cores).results

    A_log = np.asarray(L1_attn_A_log, np.float32)
    D_par = np.asarray(L1_attn_D_param, np.float32)
    inC = []
    for cid in cores:
        g, s4 = cid // 4, cid % 4
        dsl = slice(512 * s4, 512 * (s4 + 1))
        dt_sh = np.concatenate([np.asarray(rB[g * 4 + q]["dt_s"])[dsl]
                                for q in range(4)], axis=1)
        u_sh = np.concatenate([np.asarray(rB[g * 4 + q]["u_out"])[dsl]
                               for q in range(4)], axis=1)
        bc_full = np.concatenate([np.asarray(rB[g * 4 + q]["bc"])
                                  for q in range(4)], axis=1)
        inC.append(dict(dt_sh=_c(dt_sh), u_sh=_c(u_sh), bc_full=_c(bc_full),
                        A_sl=_c(A_log[dsl]),
                        D_sl=_c(D_par[dsl].reshape(4, 128).T)))
    rC = run_bass_kernel_spmd(_get("C", build_C), inC, core_ids=cores).results

    shared_D = dict(
        W_out=_c(L1_attn_W_out),
        b_out_col=_col(L1_attn_b_out, 8),
        Wa_down=_c(L1_ffn_Wa_down), ba_down=_c(L1_ffn_ba_down),
        Wa_up=_c(L1_ffn_Wa_up), ba_up=_c(L1_ffn_ba_up))
    inD = []
    for cid in cores:
        b, q = cid // 4, cid % 4
        tsl = slice(q * TOK, (q + 1) * TOK)
        Y_own = np.concatenate([np.asarray(rC[b * 4 + s4]["Y_sh"])[:, tsl]
                                for s4 in range(4)], axis=0)
        lflat = np.concatenate([np.asarray(rB[b * 4 + i]["losses"])[:, 0]
                                for i in range(4)])[None, :]
        inD.append(dict(x1fm=_c(np.asarray(rB[cid]["x1"]).T), Y_own=_c(Y_own),
                        sz=_c(rB[cid]["sz"]), base=_c(rB[cid]["base"]),
                        xn1fm=np.ascontiguousarray(np.asarray(rB[cid]["xn1fm"])),
                        lflat=_c(lflat), **shared_D))
    rD = run_bass_kernel_spmd(_get("D", build_D), inD, core_ids=cores).results

    out = np.empty((2, BT, DM), np.float32)
    for cid in cores:
        b, q = cid // 4, cid % 4
        out[b, q * TOK:(q + 1) * TOK] = np.asarray(rD[cid]["x2fm"]).T
    return out
